# version 08
# baseline (speedup 1.0000x reference)
import contextlib
import numpy as np
import concourse.bass as bass
import concourse.mybir as mybir
from concourse.bass_utils import run_bass_kernel_spmd

F32 = mybir.dt.float32
BF16 = mybir.dt.bfloat16
AF = mybir.ActivationFunctionType
ALU = mybir.AluOpType
AX = mybir.AxisListType

D = 2048
SEQ = 2048
NB = 2
T = NB * SEQ
NH = 16
DH = 128
NE = 32
FI = 768
KC = D // 128
NSW = 4
TSW = T // NSW
ISQ = 1.0 / float(np.sqrt(float(DH)))

ENGINES = ("sync", "gpsimd", "tensor", "vector", "scalar")


class Sched:

    def __init__(self):
        self.ev = {e: [] for e in ENGINES}
        self.cnt = {}

    def n(self, sem):
        return self.cnt.get(sem, 0)

    def add(self, eng, fn, waits=(), inc=None, k=1):
        w = [(s, v) for (s, v) in waits if v > 0]
        after = None
        if inc is not None:
            self.cnt[inc] = self.cnt.get(inc, 0) + k
            after = self.cnt[inc]
        self.ev[eng].append((w, fn, inc, k))
        return after

    def check(self):
        pos = {e: 0 for e in ENGINES}
        val = {}
        progress = True
        while progress:
            progress = False
            for e in ENGINES:
                while pos[e] < len(self.ev[e]):
                    waits, _fn, inc, k = self.ev[e][pos[e]]
                    if any(val.get(s, 0) < v for (s, v) in waits):
                        break
                    if inc is not None:
                        val[inc] = val.get(inc, 0) + k
                    pos[e] += 1
                    progress = True
        stuck = {e: pos[e] for e in ENGINES if pos[e] < len(self.ev[e])}
        if stuck:
            det = []
            for e in stuck:
                waits = self.ev[e][pos[e]][0]
                bad = [(s, v, val.get(s, 0)) for (s, v) in waits
                       if val.get(s, 0) < v]
                det.append(f"{e}@{pos[e]}/{len(self.ev[e])} waits {bad}")
            raise RuntimeError("schedule deadlock: " + "; ".join(det))


def build_program():
    nc = bass.Bass("TRN2", target_bir_lowering=False)

    x_ap = nc.dram_tensor("x", [NB, SEQ, D], F32, kind="ExternalInput").ap()
    wq_ap = nc.dram_tensor("wq", [D, D], F32, kind="ExternalInput").ap()
    wk_ap = nc.dram_tensor("wk", [D, D], F32, kind="ExternalInput").ap()
    wv_ap = nc.dram_tensor("wv", [D, D], F32, kind="ExternalInput").ap()
    wo_ap = nc.dram_tensor("wo", [D, D], F32, kind="ExternalInput").ap()
    bq_ap = nc.dram_tensor("bq", [D], F32, kind="ExternalInput").ap()
    bk_ap = nc.dram_tensor("bk", [D], F32, kind="ExternalInput").ap()
    bv_ap = nc.dram_tensor("bv", [D], F32, kind="ExternalInput").ap()
    bo_ap = nc.dram_tensor("bo", [D], F32, kind="ExternalInput").ap()
    gate_ap = nc.dram_tensor("gate", [D, NE], F32, kind="ExternalInput").ap()
    gup_ap = nc.dram_tensor("gup", [NE, D, 2 * FI], F32, kind="ExternalInput").ap()
    dpj_ap = nc.dram_tensor("dpj", [NE, FI, D], F32, kind="ExternalInput").ap()
    out_ap = nc.dram_tensor("out", [NB, SEQ, D], F32, kind="ExternalOutput").ap()

    ot_sc = nc.dram_tensor("ot_sc", [NB, D, SEQ], BF16, kind="Internal").ap()
    x1_sc = nc.dram_tensor("x1_sc", [T, D], F32, kind="Internal").ap()
    x1t_sc = nc.dram_tensor("x1t_sc", [D, T], BF16, kind="Internal").ap()

    es = contextlib.ExitStack()
    sched = Sched()
    sem_names = set()

    def SEM(name):
        sem_names.add(name)
        return name

    def ev(eng, fn, waits=(), inc=None, k=1):
        return sched.add(eng, fn, waits, inc, k)

    with es:
        R0 = nc.alloc_sbuf_tensor("R0", [128, 32768], BF16)
        R1 = nc.alloc_sbuf_tensor("R1", [128, 16384], F32)
        R2 = nc.alloc_sbuf_tensor("R2", [128, 8192], BF16)
        R3 = nc.alloc_sbuf_tensor("R3", [128, 18432], BF16)

        sb = lambda n, s, d: nc.alloc_sbuf_tensor(n, s, d)
        identf = sb("identf", [128, 128], F32)
        ident_bf = sb("ident_bf", [128, 128], BF16)
        ones_f = sb("ones_f", [1, 128], F32)
        ones_bf = sb("ones_bf", [1, 128], BF16)
        b16 = R1[0:16, 0:128]
        bqTs = sb("bqTs", [128, 16], F32)
        bkT = sb("bkT", [128, 16], F32)
        bvT = sb("bvT", [128, 16], F32)
        gate_sb = sb("gate_sb", [128, KC * NE], BF16)
        rw_all = sb("rw_all", [128, (T // 128) * NE], F32)
        stats = sb("stats", [128, 24], F32)
        P_un = sb("P_un", [128, 2048], BF16)
        P_un3 = sb("P_un3", [128, 2048], BF16)
        diag2 = sb("diag2", [128, 256], BF16)
        OT_sb = sb("OT_sb", [128, 1024], BF16)
        sg_sb = sb("sg_sb", [128, 1024], BF16)
        p_sb = sb("p_sb", [128, NE], F32)
        pwork = sb("pwork", [128, NE], F32)
        wun = sb("wun", [128, NE], F32)
        x1res = sb("x1res", [128, D], F32)

        PS_A = nc.alloc_psum_tensor("PS_A", [128, 2048], F32)
        PS_B = nc.alloc_psum_tensor("PS_B", [128, 2048], F32)
        SCT = (PS_A, PS_B)

        (NEGM0, NEGM1, SSUM0, SSUM1, RS0, RS1, MUS, NEGMU, VARS, LNV, RSTD,
         BIAS2, NEGML, SUME, MX, SWUN, RSW) = range(17)
        st_c = lambda i: stats[:, i:i + 1]

        s_pe = SEM("s_pe")
        s_v = SEM("s_v")
        s_sc = SEM("s_sc")
        s_g = SEM("s_g")
        s_pro = SEM("s_pro")
        s_xr = [SEM(f"s_xr{i}") for i in range(2)]
        s_wr = [SEM(f"s_wr{i}") for i in range(2)]
        s_oti = [SEM(f"s_oti{i}") for i in range(2)]
        s_ot = [SEM(f"s_ot{i}") for i in range(2)]
        s_xz = [SEM(f"s_xz{i}") for i in range(2)]
        s_sx1 = SEM("s_sx1")
        s_sxt = SEM("s_sxt")
        s_xts = SEM("s_xts")
        s_gu = [SEM(f"s_gu{i}") for i in range(4)]
        s_dp = SEM("s_dp")
        s_x1r = SEM("s_x1r")
        s_out = SEM("s_out")

        fence = {}

        def rd_fence(region, sem, cnt):
            for r in (region if isinstance(region, tuple) else (region,)):
                fence[r] = (sem, cnt)

        def get_fence(*regions):
            out = []
            for r in regions:
                if isinstance(r, tuple):
                    out.extend(fence[x] for x in r if x in fence)
                elif r in fence:
                    out.append(fence[r])
            return out

        PA01 = ("A0", "A1")
        PA23 = ("A2", "A3")
        PB01 = ("B0", "B1")
        PB23 = ("B2", "B3")
        PAIR01 = (PA01, PB01)
        PAIR23 = (PA23, PB23)

        def transpose_rows(src_tile, dest_fn, waits_ld):
            pe_last = 0
            for g in range(2):
                region = PA01 if g == 0 else PA23
                def mm(e, g=g, src=src_tile):
                    last = None
                    for j in range(8):
                        dc = 8 * g + j
                        last = e.matmul(
                            PS_A[:, 1024 * g + 128 * j:1024 * g + 128 * (j + 1)],
                            src[:, 128 * dc:128 * (dc + 1)],
                            ident_bf[:, :], start=True, stop=True)
                    return last
                ev("tensor", mm, waits=list(waits_ld) + get_fence(region),
                   inc=s_pe)
                pe = sched.n(s_pe)
                pe_last = pe
                def evac(e, g=g, dest_fn=dest_fn):
                    return e.copy(dest_fn(g),
                                  PS_A[:, 1024 * g:1024 * (g + 1)]
                                  .rearrange("p (k t) -> p k t", t=128))
                ev("scalar", evac, waits=[(s_pe, pe)], inc=s_sc)
                rd_fence(region, s_sc, sched.n(s_sc))
            return pe_last

        ev("gpsimd", lambda e: e.memset(identf[:, :], 0.0), inc=s_g)
        ev("gpsimd", lambda e: e.affine_select(
            out=identf[:, :], in_=identf[:, :], compare_op=ALU.not_equal,
            fill=1.0, base=0, pattern=[[-1, 128]], channel_multiplier=1),
           waits=[(s_g, sched.n(s_g))], inc=s_g)
        ev("gpsimd", lambda e: e.memset(ones_f[:, :], 1.0),
           waits=[(s_g, sched.n(s_g))], inc=s_g)
        g_done = sched.n(s_g)
        ev("scalar", lambda e: e.copy(ident_bf[:, :], identf[:, :]),
           waits=[(s_g, g_done)], inc=s_sc)
        ev("scalar", lambda e: e.copy(ones_bf[:, :], ones_f[:, :]), inc=s_sc)
        sc_idents = sched.n(s_sc)

        for src, dst, scale in ((bq_ap, bqTs, ISQ), (bk_ap, bkT, 1.0),
                                (bv_ap, bvT, 1.0)):
            ev("sync", lambda e, s=src: e.dma_start(
                out=b16[:, :], in_=s.rearrange("(a b) -> a b", b=128)),
               waits=[(s_sc, sched.n(s_sc))], inc=s_pro, k=16)
            ld = sched.n(s_pro)
            ev("tensor", lambda e: e.matmul(
                PS_A[:, 0:16], b16[:, :], identf[0:16, 0:16],
                start=True, stop=True),
               waits=[(s_pro, ld), (s_g, g_done)] + get_fence("A0"), inc=s_pe)
            pe = sched.n(s_pe)
            ev("scalar", lambda e, d=dst, sc=scale: e.activation(
                d[:, :], PS_A[:, 0:16], AF.Identity, bias=0.0, scale=sc),
               waits=[(s_pe, pe)], inc=s_sc)
            rd_fence("A0", s_sc, sched.n(s_sc))
        ev("gpsimd", lambda e: e.dma_start(
            out=gate_sb[:, :].rearrange("p (k e) -> p k e", e=NE),
            in_=gate_ap.rearrange("(k p) e -> p k e", p=128)),
           inc=s_g, k=16)
        pro_done_g = sched.n(s_g)

        XT = R0
        KT = R2[:, 0:2048]
        Vh = R2[:, 2048:4096]
        QT = R2[:, 4096:6144]
        PUN = (P_un, R2[:, 6144:8192], P_un3)
        DIAG = (diag2[:, 0:128], diag2[:, 128:256])
        wT = R3[:, 0:2048]
        xw = [R3[:, 2048:4096], R3[:, 4096:6144]]
        panel = (R3[:, 2048:10240], R3[:, 10240:18432])

        def p1a(b):
            for tc in range(KC):
                buf = tc % 2
                ev("gpsimd", lambda e, b=b, tc=tc, buf=buf: e.dma_start(
                    out=xw[buf][:, :],
                    in_=x_ap[b, 128 * tc:128 * (tc + 1), :]),
                   waits=get_fence(f"xw{buf}", "PAN0")
                   + [(s_sc, sc_idents), (s_xr[buf], sched.n(s_xr[buf]))],
                   inc=s_xr[buf], k=16)
                ld = sched.n(s_xr[buf])
                def dest(g, tc=tc):
                    o = XT[:, :].rearrange("p (k t) -> p k t", t=2048)
                    return o[:, 8 * g:8 * (g + 1), 128 * tc:128 * (tc + 1)]
                pe = transpose_rows(xw[buf], dest,
                                    [(s_xr[buf], ld), (s_sc, sc_idents)]
                                    + get_fence("XTw"))
                rd_fence(f"xw{buf}", s_pe, pe)

        def build_wT(w_ap, h):
            buf = h % 2
            ev("gpsimd", lambda e, h=h, buf=buf, w_ap=w_ap: e.dma_start(
                out=xw[buf][:, :],
                in_=w_ap[128 * h:128 * (h + 1), :]),
               waits=get_fence(f"xw{buf}", "PAN0")
               + [(s_wr[buf], sched.n(s_wr[buf]))],
               inc=s_wr[buf], k=16)
            ld = sched.n(s_wr[buf])
            def dest(g):
                o = wT.rearrange("p (k f) -> p k f", f=128)
                return o[:, 8 * g:8 * (g + 1), :]
            pe = transpose_rows(xw[buf], dest, [(s_wr[buf], ld)]
                                + get_fence("wTw"))
            rd_fence(f"xw{buf}", s_pe, pe)
            return sched.n(s_sc)

        def proj_T(h, bias_t, scale, out_t):
            wT_ready = sched.n(s_sc)
            for tn in range(4):
                rr = tn % 4
                c0 = 512 * rr
                def mm(e, tn=tn, c0=c0):
                    last = None
                    for k in range(KC):
                        last = e.matmul(
                            PS_B[:, c0:c0 + 512],
                            wT[:, 128 * k:128 * (k + 1)],
                            XT[:, 2048 * k + 512 * tn:2048 * k + 512 * (tn + 1)],
                            start=(k == 0), stop=(k == KC - 1))
                    return last
                ev("tensor", mm,
                   waits=[(s_sc, wT_ready)] + get_fence(f"B{rr}"), inc=s_pe)
                pe = sched.n(s_pe)
                def evac(e, tn=tn, c0=c0, h=h, bias_t=bias_t, scale=scale,
                         out_t=out_t):
                    return e.activation(
                        out_t[:, 512 * tn:512 * (tn + 1)],
                        PS_B[:, c0:c0 + 512], AF.Identity,
                        bias=bias_t[:, h:h + 1], scale=scale)
                ev("scalar", evac,
                   waits=[(s_pe, pe)] + get_fence("KVQ"), inc=s_sc)
                rd_fence(f"B{rr}", s_sc, sched.n(s_sc))
            rd_fence("wTw", s_pe, sched.n(s_pe))

        def proj_V(h):
            wT_ready = sched.n(s_sc)
            for tc in range(KC):
                rr = tc % 4
                c0 = 512 * rr
                def mm(e, tc=tc, c0=c0):
                    last = None
                    for k in range(KC):
                        last = e.matmul(
                            PS_B[:, c0:c0 + 128],
                            XT[:, 2048 * k + 128 * tc:2048 * k + 128 * (tc + 1)],
                            wT[:, 128 * k:128 * (k + 1)],
                            start=(k == 0), stop=(k == KC - 1))
                    return last
                ev("tensor", mm,
                   waits=[(s_sc, wT_ready)] + get_fence(f"B{rr}"), inc=s_pe)
                pe = sched.n(s_pe)
                ev("scalar", lambda e, tc=tc, c0=c0: e.copy(
                    Vh[:, 128 * tc:128 * (tc + 1)], PS_B[:, c0:c0 + 128]),
                   waits=[(s_pe, pe)] + get_fence("KVQ"), inc=s_sc)
                rd_fence(f"B{rr}", s_sc, sched.n(s_sc))
            rd_fence("wTw", s_pe, sched.n(s_pe))

        def p1c(b, h):
            kvq_ready = sched.n(s_sc)
            negm_v = {}
            diag_g = {}
            ev01_v = {}
            ev23_sc = {}

            def emit_mms(i):
                p = i % 2
                SC = SCT[p]
                w1 = [(s_sc, kvq_ready)] + get_fence(PAIR01[p])
                def mm01(e, i=i, SC=SC):
                    last = None
                    for kn in range(2):
                        last = e.matmul(
                            SC[:, 512 * kn:512 * (kn + 1)],
                            QT[:, 128 * i:128 * (i + 1)],
                            KT[:, 512 * kn:512 * (kn + 1)],
                            start=True, stop=True)
                    return last
                ev("tensor", mm01, waits=w1, inc=s_pe)
                w2 = get_fence(PAIR23[p])
                def mm23(e, i=i, SC=SC):
                    last = None
                    for kn in range(2, 4):
                        last = e.matmul(
                            SC[:, 512 * kn:512 * (kn + 1)],
                            QT[:, 128 * i:128 * (i + 1)],
                            KT[:, 512 * kn:512 * (kn + 1)],
                            start=True, stop=True)
                    return last
                ev("tensor", mm23, waits=w2, inc=s_pe)
                return sched.n(s_pe)

            def emit_negm(i, mms_pe):
                p = i % 2
                negm_c = NEGM0 if p == 0 else NEGM1
                ev("vector", lambda e, SC=SCT[p], negm_c=negm_c: e.reduce_max(
                    st_c(negm_c), SC[:, :], axis=AX.X, negate=True),
                   waits=[(s_pe, mms_pe)] + get_fence(f"NEGM{p}"), inc=s_v)
                negm_v[i] = sched.n(s_v)

            def emit_exp(i):
                p = i % 2
                r3 = i % 3
                negm_c, ssum_c = (NEGM0, SSUM0) if p == 0 else (NEGM1, SSUM1)
                ev("scalar", lambda e, r3=r3, SC=SCT[p], negm_c=negm_c,
                   ssum_c=ssum_c: e.activation(
                       PUN[r3][:, :], SC[:, :], AF.Exp, bias=st_c(negm_c),
                       scale=1.0, accum_out=st_c(ssum_c)),
                   waits=[(s_v, negm_v[i])]
                   + get_fence(f"PUN{r3}", f"SSUM{p}"),
                   inc=s_sc)
                n_exp = sched.n(s_sc)
                rd_fence(f"NEGM{p}", s_sc, n_exp)
                rd_fence(PAIR01[p], s_sc, n_exp)
                rd_fence(PAIR23[p], s_sc, n_exp)
                return n_exp

            def emit_rcd(i, n_exp):
                p = i % 2
                ssum_c, rs_c = (SSUM0, RS0) if p == 0 else (SSUM1, RS1)
                ev("vector", lambda e, ssum_c=ssum_c, rs_c=rs_c: e.reciprocal(
                    st_c(rs_c), st_c(ssum_c)),
                   waits=[(s_sc, n_exp)] + get_fence(f"RS{p}"), inc=s_v)
                n_rc = sched.n(s_v)
                rd_fence(f"SSUM{p}", s_v, n_rc)
                ev("gpsimd", lambda e, p=p, rs_c=rs_c: e.tensor_scalar(
                    DIAG[p], identf[:, :], st_c(rs_c), None, ALU.mult),
                   waits=[(s_v, n_rc), (s_g, g_done)]
                   + get_fence(f"DIAG{p}"), inc=s_g)
                diag_g[i] = sched.n(s_g)
                rd_fence(f"RS{p}", s_g, diag_g[i])

            def emit_ptg0(j):
                q = (j + 1) % 2
                r3 = j % 3
                def ptg0(e, SC=SCT[q], r3=r3, p=j % 2):
                    last = None
                    for c in range(8):
                        last = e.matmul(
                            SC[:, 128 * c:128 * (c + 1)],
                            PUN[r3][:, 128 * c:128 * (c + 1)],
                            DIAG[p], start=True, stop=True)
                    return last
                ev("tensor", ptg0,
                   waits=[(s_g, diag_g[j])] + get_fence(PAIR01[q]), inc=s_pe)
                return sched.n(s_pe)

            def emit_ev01(j, pe_g0):
                q = (j + 1) % 2
                pan = (j // 4) % 2
                qr = j % 4
                panw = get_fence(f"PAN{pan}")
                if pan == 0:
                    panw = panw + get_fence("xw0", "xw1")
                def ev01(e, SC=SCT[q], pan=pan, qr=qr):
                    o = panel[pan].rearrange("p (kc q) -> p kc q", q=512)
                    return e.tensor_copy(
                        o[:, 0:8, 128 * qr:128 * (qr + 1)],
                        SC[:, 0:1024].rearrange("p (kc q) -> p kc q", q=128))
                ev("vector", ev01, waits=[(s_pe, pe_g0)] + panw, inc=s_v)
                ev01_v[j] = sched.n(s_v)
                rd_fence(PAIR01[q], s_v, ev01_v[j])

            def emit_ptg1(j):
                q = (j + 1) % 2
                r3 = j % 3
                def ptg1(e, SC=SCT[q], r3=r3, p=j % 2):
                    last = None
                    for c in range(8, 16):
                        last = e.matmul(
                            SC[:, 128 * (c - 8):128 * (c - 7)],
                            PUN[r3][:, 128 * c:128 * (c + 1)],
                            DIAG[p], start=True, stop=True)
                    return last
                ev("tensor", ptg1,
                   waits=[(s_g, diag_g[j])] + get_fence(PAIR01[q]), inc=s_pe)
                pe_g1 = sched.n(s_pe)
                rd_fence(f"PUN{j % 3}", s_pe, pe_g1)
                rd_fence(f"DIAG{j % 2}", s_pe, pe_g1)
                return pe_g1

            def emit_ev23(j, pe_g1):
                q = (j + 1) % 2
                pan = (j // 4) % 2
                qr = j % 4
                panw = get_fence(f"PAN{pan}")
                if pan == 0:
                    panw = panw + get_fence("xw0", "xw1")
                def ev23(e, SC=SCT[q], pan=pan, qr=qr):
                    o = panel[pan].rearrange("p (kc q) -> p kc q", q=512)
                    return e.copy(
                        o[:, 8:16, 128 * qr:128 * (qr + 1)],
                        SC[:, 0:1024].rearrange("p (kc q) -> p kc q", q=128))
                ev("scalar", ev23, waits=[(s_pe, pe_g1)] + panw, inc=s_sc)
                ev23_sc[j] = sched.n(s_sc)
                rd_fence(PAIR01[q], s_sc, ev23_sc[j])

            def emit_mmo(a):
                pan = a % 2
                SC = SCT[0]
                def mmo(e, SC=SC, pan=pan):
                    last = None
                    for tc in range(KC):
                        last = e.matmul(
                            SC[:, 1024:1536],
                            Vh[:, 128 * tc:128 * (tc + 1)],
                            panel[pan][:, 512 * tc:512 * (tc + 1)],
                            start=(tc == 0), stop=(tc == KC - 1))
                    return last
                ev("tensor", mmo,
                   waits=[(s_v, ev01_v[4 * a + 3]),
                          (s_sc, ev23_sc[4 * a + 3]),
                          (s_sc, kvq_ready)] + get_fence(PAIR23[0]),
                   inc=s_pe)
                mmo_pe = sched.n(s_pe)
                rd_fence(f"PAN{pan}", s_pe, mmo_pe)
                return mmo_pe

            def emit_ot(a, mmo_pe):
                ping = a % 2
                nst = sched.n(s_oti[ping])
                ev("scalar", lambda e, SC=SCT[0], ping=ping, h=h:
                   e.activation(
                       OT_sb[:, 512 * ping:512 * (ping + 1)],
                       SC[:, 1024:1536],
                       AF.Identity, bias=bvT[:, h:h + 1], scale=1.0),
                   waits=[(s_pe, mmo_pe), (s_oti[ping], nst)], inc=s_sc)
                sc_oe = sched.n(s_sc)
                rd_fence(PAIR23[0], s_sc, sc_oe)
                ev("sync", lambda e, b=b, h=h, a=a, ping=ping: e.dma_start(
                    out=ot_sc[b, 128 * h:128 * (h + 1),
                              512 * a:512 * (a + 1)],
                    in_=OT_sb[:, 512 * ping:512 * (ping + 1)]),
                   waits=[(s_sc, sc_oe)], inc=s_oti[ping], k=16)

            for i in range(16):
                mms_pe = emit_mms(i)
                pe_g0 = emit_ptg0(i - 2) if i >= 2 else None
                emit_negm(i, mms_pe)
                if i >= 2:
                    emit_ev01(i - 2, pe_g0)
                    pe_g1 = emit_ptg1(i - 2)
                n_exp = emit_exp(i)
                if i >= 2:
                    emit_ev23(i - 2, pe_g1)
                mmo_due = i >= 7 and (i - 7) % 4 == 0
                if mmo_due:
                    a = (i - 7) // 4
                    mmo_pe = emit_mmo(a)
                emit_rcd(i, n_exp)
                if mmo_due:
                    emit_ot(a, mmo_pe)
            for j in (14, 15):
                pe_g0 = emit_ptg0(j)
                emit_ev01(j, pe_g0)
                pe_g1 = emit_ptg1(j)
                emit_ev23(j, pe_g1)
            mmo_pe = emit_mmo(3)
            emit_ot(3, mmo_pe)
            rd_fence("KVQ", s_pe, sched.n(s_pe))

        for b in range(NB):
            p1a(b)
            for h in range(NH):
                build_wT(wk_ap, h)
                proj_T(h, bkT, 1.0, KT)
                build_wT(wv_ap, h)
                proj_V(h)
                build_wT(wq_ap, h)
                proj_T(h, bqTs, ISQ, QT)
                p1c(b, h)
            rd_fence("XTw", s_pe, sched.n(s_pe))

        woT = R0
        ot_t = [R2[:, 0:2048], R2[:, 2048:4096]]
        x1b = R2[:, 4096:6144]
        x1t_sb = R2[:, 6144:8192]
        x_res = [R1[:, 0:2048], R1[:, 2048:4096]]
        h_sb = R1[:, 4096:6144]
        x1f = R1[:, 6144:8192]
        borow = R3[0:1, 0:2048]

        def emit_ln(xin, xout, sq_waits, in_ready):
            ev("vector", lambda e: e.reduce_sum(st_c(MUS), xin[:, :],
                                                axis=AX.X),
               waits=list(in_ready), inc=s_v)
            v1 = sched.n(s_v)
            ev("vector", lambda e: e.tensor_scalar(
                st_c(NEGMU), st_c(MUS), -1.0 / D, None, ALU.mult),
               waits=[(s_v, v1)], inc=s_v)
            v2 = sched.n(s_v)
            ev("scalar", lambda e: e.activation(
                xout[:, :], xin[:, :], AF.Square, bias=st_c(NEGMU),
                scale=1.0, accum_out=st_c(VARS)),
               waits=[(s_v, v2)] + list(sq_waits), inc=s_sc)
            s1 = sched.n(s_sc)
            ev("vector", lambda e: e.tensor_scalar(
                st_c(LNV), st_c(VARS), 1.0 / D, 1e-5, ALU.mult, ALU.add),
               waits=[(s_sc, s1)], inc=s_v)
            v3 = sched.n(s_v)
            ev("scalar", lambda e: e.activation(st_c(RSTD), st_c(LNV), AF.Ln),
               waits=[(s_v, v3)], inc=s_sc)
            s2 = sched.n(s_sc)
            ev("scalar", lambda e: e.activation(st_c(RSTD), st_c(RSTD),
                                                AF.Exp, scale=-0.5),
               waits=[(s_sc, s2)], inc=s_sc)
            s3 = sched.n(s_sc)
            ev("vector", lambda e: e.tensor_tensor(
                st_c(BIAS2), st_c(NEGMU), st_c(RSTD), ALU.mult),
               waits=[(s_sc, s3), (s_v, v3)], inc=s_v)
            v4 = sched.n(s_v)
            ev("scalar", lambda e: e.activation(
                xout[:, :], xin[:, :], AF.Identity, bias=st_c(BIAS2),
                scale=st_c(RSTD)),
               waits=[(s_v, v4), (s_sc, s3)], inc=s_sc)
            return sched.n(s_sc)

        ev("gpsimd", lambda e: e.dma_start(out=borow, in_=bo_ap[None, :]),
           waits=[(s_pe, sched.n(s_pe))], inc=s_g, k=16)
        bo_ready = sched.n(s_g)

        for m in range(KC):
            buf = m % 2
            ev("gpsimd", lambda e, m=m, buf=buf: e.dma_start(
                out=xw[buf][:, :], in_=wo_ap[128 * m:128 * (m + 1), :]),
               waits=get_fence(f"xw{buf}", "PAN0")
               + [(s_wr[buf], sched.n(s_wr[buf]))],
               inc=s_wr[buf], k=16)
            ld = sched.n(s_wr[buf])
            def dest(g, m=m):
                o = woT[:, :].rearrange("p (k f) -> p k f", f=2048)
                return o[:, 8 * g:8 * (g + 1), 128 * m:128 * (m + 1)]
            pe = transpose_rows(xw[buf], dest, [(s_wr[buf], ld)])
            rd_fence(f"xw{buf}", s_pe, pe)
        woT_ready = sched.n(s_sc)

        ntile = T // 128
        tile_st = {}

        def p2_stage_a(tt):
            b, trow = tt // (SEQ // 128), 128 * (tt % (SEQ // 128))
            buf = tt % 2
            ev("sync", lambda e, b=b, trow=trow, buf=buf: e.dma_start(
                out=ot_t[buf][:, :].rearrange("p (k q) -> p k q", q=128),
                in_=ot_sc[b, :, trow:trow + 128]
                .rearrange("(k p) q -> p k q", p=128)),
               waits=get_fence(f"ot_t{buf}") + [(s_ot[buf],
                                                 sched.n(s_ot[buf]))],
               inc=s_ot[buf], k=16)
            ld_ot = sched.n(s_ot[buf])
            ev("scalar", lambda e, b=b, trow=trow, buf=buf: e.dma_start(
                out=x_res[buf][:, :], in_=x_ap[b, trow:trow + 128, :]),
               waits=get_fence(f"x_res{buf}") + [(s_xz[buf],
                                                  sched.n(s_xz[buf]))],
               inc=s_xz[buf], k=16)
            ld_xr = sched.n(s_xz[buf])
            v_last = 0
            for n in range(4):
                def mm_a(e, buf=buf, n=n):
                    e.matmul(PS_A[:, 512 * n:512 * (n + 1)], ones_bf[0:1, :],
                             borow[0:1, 512 * n:512 * (n + 1)],
                             start=True, stop=False)
                    last = None
                    for k in range(KC):
                        last = e.matmul(
                            PS_A[:, 512 * n:512 * (n + 1)],
                            ot_t[buf][:, 128 * k:128 * (k + 1)],
                            woT[:, 2048 * k + 512 * n:2048 * k + 512 * (n + 1)],
                            start=False, stop=(k == KC - 1))
                    return last
                ev("tensor", mm_a,
                   waits=[(s_ot[buf], ld_ot), (s_sc, woT_ready),
                          (s_g, bo_ready)] + get_fence(f"A{n}"), inc=s_pe)
                pe_a = sched.n(s_pe)
                ev("vector", lambda e, buf=buf, n=n: e.tensor_tensor(
                    h_sb[:, 512 * n:512 * (n + 1)],
                    PS_A[:, 512 * n:512 * (n + 1)],
                    x_res[buf][:, 512 * n:512 * (n + 1)], ALU.add),
                   waits=[(s_pe, pe_a), (s_xz[buf], ld_xr)]
                         + get_fence("h_sb"), inc=s_v)
                v_last = sched.n(s_v)
                rd_fence(f"A{n}", s_v, v_last)
            rd_fence(f"ot_t{buf}", s_pe, sched.n(s_pe))
            rd_fence(f"x_res{buf}", s_v, v_last)
            sc_x1 = emit_ln(h_sb, x1f,
                            sq_waits=get_fence("x1f_dma", "x1f_v"),
                            in_ready=[(s_v, v_last)])
            rd_fence("h_sb", s_sc, sc_x1)
            ev("vector", lambda e: e.tensor_copy(x1b[:, :], x1f[:, :]),
               waits=[(s_sc, sc_x1)] + get_fence("x1b"), inc=s_v)
            v_x1b = sched.n(s_v)
            rd_fence("x1f_v", s_v, v_x1b)
            ev("sync", lambda e, tt=tt: e.dma_start(
                out=x1_sc[128 * tt:128 * (tt + 1), :], in_=x1f[:, :]),
               waits=[(s_sc, sc_x1), (s_sx1, sched.n(s_sx1))],
               inc=s_sx1, k=16)
            rd_fence("x1f_dma", s_sx1, sched.n(s_sx1))
            tile_st[tt] = v_x1b

        def p2_stage_b(tt):
            v_x1b = tile_st.pop(tt)
            for g in range(2):
                region = PB01 if g == 0 else PB23
                def mmt(e, g=g):
                    last = None
                    for j in range(8):
                        dc = 8 * g + j
                        last = e.matmul(
                            PS_B[:, 1024 * g + 128 * j:1024 * g + 128 * (j + 1)],
                            x1b[:, 128 * dc:128 * (dc + 1)],
                            ident_bf[:, :], start=True, stop=True)
                    return last
                ev("tensor", mmt, waits=[(s_v, v_x1b)] + get_fence(region),
                   inc=s_pe)
                pe_t = sched.n(s_pe)
                ev("scalar", lambda e, g=g: e.copy(
                    x1t_sb[:, 1024 * g:1024 * (g + 1)],
                    PS_B[:, 1024 * g:1024 * (g + 1)]),
                   waits=[(s_pe, pe_t)] + get_fence("x1t_dma", "x1t_pe"),
                   inc=s_sc)
                rd_fence(region, s_sc, sched.n(s_sc))
            sc_x1t = sched.n(s_sc)
            rd_fence("x1b", s_pe, sched.n(s_pe))
            ev("scalar", lambda e, tt=tt: e.dma_start(
                out=x1t_sc[:, 128 * tt:128 * (tt + 1)]
                .rearrange("(k p) t -> p k t", p=128),
                in_=x1t_sb[:, :].rearrange("p (k t) -> p k t", t=128)),
               waits=[(s_sc, sc_x1t), (s_sxt, sched.n(s_sxt))],
               inc=s_sxt, k=16)
            rd_fence("x1t_dma", s_sxt, sched.n(s_sxt))
            def mm_r(e):
                last = None
                for k in range(KC):
                    last = e.matmul(
                        PS_B[:, 0:NE], x1t_sb[:, 128 * k:128 * (k + 1)],
                        gate_sb[:, NE * k:NE * (k + 1)],
                        start=(k == 0), stop=(k == KC - 1))
                return last
            ev("tensor", mm_r,
               waits=[(s_sc, sc_x1t), (s_g, pro_done_g)] + get_fence("B0"),
               inc=s_pe)
            pe_r = sched.n(s_pe)
            rd_fence("x1t_pe", s_pe, pe_r)
            ev("vector", lambda e: e.reduce_max(
                st_c(NEGML), PS_B[:, 0:NE], axis=AX.X, negate=True),
               waits=[(s_pe, pe_r)], inc=s_v)
            v_ml = sched.n(s_v)
            ev("scalar", lambda e: e.activation(
                p_sb[:, :], PS_B[:, 0:NE], AF.Exp, bias=st_c(NEGML),
                accum_out=st_c(SUME)),
               waits=[(s_v, v_ml)], inc=s_sc)
            sc_pr = sched.n(s_sc)
            rd_fence("B0", s_sc, sc_pr)
            def vstep(fn, extra=()):
                ev("vector", fn, waits=[(s_v, sched.n(s_v))] + list(extra),
                   inc=s_v)
            vstep(lambda e: e.tensor_copy(pwork[:, :], p_sb[:, :]),
                  extra=[(s_sc, sc_pr)])
            for _r in range(4):
                vstep(lambda e: e.reduce_max(st_c(MX), pwork[:, :],
                                             axis=AX.X))
                vstep(lambda e: e.tensor_scalar(
                    wun[:, :], pwork[:, :], st_c(MX), None, ALU.is_equal))
                vstep(lambda e: e.scalar_tensor_tensor(
                    pwork[:, :], wun[:, :], -2.0, pwork[:, :],
                    ALU.mult, ALU.add))
            vstep(lambda e: e.scalar_tensor_tensor(
                wun[:, :], p_sb[:, :], st_c(MX), p_sb[:, :],
                ALU.is_ge, ALU.mult))
            vstep(lambda e: e.reduce_sum(st_c(SWUN), wun[:, :], axis=AX.X))
            vstep(lambda e: e.reciprocal(st_c(RSW), st_c(SWUN)))
            vstep(lambda e, tt=tt: e.tensor_scalar(
                rw_all[:, NE * tt:NE * (tt + 1)], wun[:, :], st_c(RSW),
                None, ALU.mult))

        for tt in range(ntile):
            p2_stage_a(tt)
            if tt >= 1:
                p2_stage_b(tt - 1)
        p2_stage_b(ntile - 1)

        ACC = R1
        dp_sb = R0[:, 0:12288]
        gup_buf = [R0[:, 12288 + 2048 * i:12288 + 2048 * (i + 1)]
                   for i in range(4)]
        Hh = R0[:, 20480:26624]
        X1T_s = R3[:, 0:8192]
        X1T_s2 = R2[:, 0:8192]

        def x1t_chunk(k):
            if k < 8:
                return X1T_s[:, 1024 * k:1024 * (k + 1)]
            return X1T_s2[:, 1024 * (k - 8):1024 * (k - 7)]

        p3_pe = sched.n(s_pe)
        p3_sc = sched.n(s_sc)
        p3_v = sched.n(s_v)
        p3_sx1 = sched.n(s_sx1)
        arena_free = [(s_pe, p3_pe), (s_sc, p3_sc), (s_v, p3_v),
                      (s_sx1, p3_sx1), (s_sxt, sched.n(s_sxt))]

        acc_fin_hist = {}
        g_last_pe = [0]
        x1res_free = [0]
        slot_free_pe = [0, 0, 0, 0]
        d_done_pe = p3_pe
        first_loads = True

        def emit_epilogue_tile(se, tsub):
            eb, er0 = se // 2, TSW * (se % 2)
            ev("scalar", lambda e, se=se, tsub=tsub: e.dma_start(
                out=x1res[:, :],
                in_=x1_sc[TSW * se + 128 * tsub:
                          TSW * se + 128 * (tsub + 1), :]),
               waits=[(s_x1r, sched.n(s_x1r)), (s_sc, x1res_free[0]),
                      (s_sx1, 16 * (8 * se + tsub + 1))],
               inc=s_x1r, k=16)
            ld_r = sched.n(s_x1r)
            accsl = ACC[:, 2048 * tsub:2048 * (tsub + 1)]
            ev("vector", lambda e, accsl=accsl: e.tensor_tensor(
                x1res[:, :], x1res[:, :], accsl, ALU.add),
               waits=[(s_x1r, ld_r), (s_v, acc_fin_hist[(se, tsub)])],
               inc=s_v)
            v_add = sched.n(s_v)
            sc_o = emit_ln(x1res, accsl, sq_waits=[],
                           in_ready=[(s_v, v_add)])
            x1res_free[0] = sc_o
            ev("scalar", lambda e, eb=eb, er0=er0, tsub=tsub, accsl=accsl:
               e.dma_start(
                   out=out_ap[eb, er0 + 128 * tsub:
                              er0 + 128 * (tsub + 1), :],
                   in_=accsl),
               waits=[(s_sc, sc_o)], inc=s_out, k=16)
            assert sched.n(s_out) == 16 * (8 * se + tsub + 1)

        for s in range(NSW):
            b, r0 = s // 2, TSW * (s % 2)
            war = list(arena_free) + [(s_pe, g_last_pe[0]),
                                      (s_xts, sched.n(s_xts))]
            xts_base = sched.n(s_xts)
            for kk in range(KC):
                ev("sync", lambda e, s=s, kk=kk: e.dma_start(
                    out=x1t_chunk(kk),
                    in_=x1t_sc[128 * kk:128 * (kk + 1),
                               TSW * s:TSW * (s + 1)]),
                   waits=war if kk == 0 else [], inc=s_xts, k=16)
            ld_xts = sched.n(s_xts)
            for e_i in range(NE):
                ldg = {}
                fine = (e_i == 0)
                per_unit = 16 if fine else 1
                pe_base2 = [0]

                def g_done_cnt(m):
                    return pe_base2[0] + per_unit * (m + 1)

                def load_chunk(c, e_i=e_i):
                    ri = c % 4
                    fcol = 128 * (c // 2) + FI * (c % 2)
                    if c < 4:
                        slot_war = slot_free_pe[ri]
                    else:
                        slot_war = g_done_cnt(c - 4)
                    w = [(s_pe, slot_war),
                         (s_gu[ri], sched.n(s_gu[ri]))]
                    if first_loads:
                        w = w + arena_free
                    ev("gpsimd", lambda e, e_i=e_i, fcol=fcol, ri=ri:
                       e.dma_start(
                           out=gup_buf[ri][:, :]
                           .rearrange("p (k f) -> p k f", f=128),
                           in_=gup_ap[e_i, :, fcol:fcol + 128]
                           .rearrange("(k p) f -> p k f", p=128)),
                       waits=w, inc=s_gu[ri], k=16)
                    ldg[c] = sched.n(s_gu[ri])

                for c in range(4):
                    load_chunk(c)
                dpw = [(s_pe, d_done_pe), (s_dp, sched.n(s_dp))]
                if first_loads:
                    dpw = dpw + arena_free
                    first_loads = False
                ev("gpsimd", lambda e, e_i=e_i: e.dma_start(
                    out=dp_sb[:, :].rearrange("p (j d) -> p j d", d=D),
                    in_=dpj_ap[e_i].rearrange("(j p) d -> p j d", p=128)),
                   waits=dpw, inc=s_dp, k=16)
                ld_dp = sched.n(s_dp)

                def emit_g_unit(unit, psl, ld_cnt, ri, fine):
                    if not fine:
                        def mm(e, psl=psl, ri=ri):
                            last = None
                            for th in range(2):
                                for k in range(KC):
                                    last = e.matmul(
                                        psl[:, 512 * th:512 * (th + 1)],
                                        gup_buf[ri][:, 128 * k:128 * (k + 1)],
                                        x1t_chunk(k)[:, 512 * th:
                                                     512 * (th + 1)],
                                        start=(k == 0), stop=(k == KC - 1))
                            return last
                        ev("tensor", mm,
                           waits=[(s_gu[ri], ld_cnt), (s_xts, ld_xts)]
                                 + get_fence(f"G{unit}"), inc=s_pe)
                    else:
                        for k in range(KC):
                            w = [(s_xts, xts_base + 16 * (k + 1))]
                            if k == 0:
                                w += [(s_gu[ri], ld_cnt)] \
                                    + get_fence(f"G{unit}")
                            def mm(e, psl=psl, ri=ri, k=k):
                                last = None
                                for th in range(2):
                                    last = e.matmul(
                                        psl[:, 512 * th:512 * (th + 1)],
                                        gup_buf[ri][:, 128 * k:128 * (k + 1)],
                                        x1t_chunk(k)[:, 512 * th:
                                                     512 * (th + 1)],
                                        start=(k == 0), stop=(k == KC - 1))
                                return last
                            ev("tensor", mm, waits=w, inc=s_pe)
                    return sched.n(s_pe)

                pe_base2[0] = sched.n(s_pe)
                for c in range(4, 12):
                    load_chunk(c)

                for j in range(6):
                    rig, riu = (2 * j) % 4, (2 * j + 1) % 4
                    pe_gg = emit_g_unit("g", PS_A[:, 0:1024], ldg[2 * j],
                                        rig, fine)
                    assert pe_gg == g_done_cnt(2 * j)
                    slot_free_pe[rig] = pe_gg
                    ev("scalar", lambda e: e.activation(
                        sg_sb[:, :], PS_A[:, 0:1024], AF.Silu),
                       waits=[(s_pe, pe_gg)] + get_fence("sg"), inc=s_sc)
                    sc_si = sched.n(s_sc)
                    rd_fence("Gg", s_sc, sc_si)
                    pe_gu = emit_g_unit("u", PS_A[:, 1024:2048],
                                        ldg[2 * j + 1], riu, fine)
                    assert pe_gu == g_done_cnt(2 * j + 1)
                    slot_free_pe[riu] = pe_gu
                    ev("vector", lambda e, j=j: e.scalar_tensor_tensor(
                        Hh[:, 1024 * j:1024 * (j + 1)],
                        PS_A[:, 1024:2048], 1.0, sg_sb[:, :],
                        ALU.mult, ALU.mult),
                       waits=[(s_sc, sc_si), (s_pe, pe_gu)], inc=s_v)
                    v_h = sched.n(s_v)
                    rd_fence("Gu", s_v, v_h)
                    rd_fence("sg", s_v, v_h)
                h_ready = sched.n(s_v)

                for u in range(16):
                    tsub, dch = u // 2, u % 2
                    if e_i == 0 and s >= 1 and dch == 0:
                        emit_epilogue_tile(s - 1, tsub)
                    par = u % 2
                    b0 = 1024 * par
                    region = f"D{par}"
                    def mm_d(e, tsub=tsub, dch=dch, b0=b0):
                        last = None
                        for j in range(6):
                            for nn in range(2):
                                last = e.matmul(
                                    PS_B[:, b0 + 512 * nn:b0 + 512 * (nn + 1)],
                                    Hh[:, 1024 * j + 128 * tsub:
                                       1024 * j + 128 * (tsub + 1)],
                                    dp_sb[:, 2048 * j + 1024 * dch + 512 * nn:
                                          2048 * j + 1024 * dch + 512 * (nn + 1)],
                                    start=(j == 0), stop=(j == 5))
                        return last
                    dwaits = [(s_dp, ld_dp), (s_v, h_ready)]
                    if e_i == 0 and s == 0:
                        dwaits = dwaits + [(s_sc, p3_sc)]
                    ev("tensor", mm_d, waits=dwaits + get_fence(region),
                       inc=s_pe)
                    pe_d = sched.n(s_pe)
                    rwcol = NE * (8 * s + tsub) + e_i
                    accsl = ACC[:, 2048 * tsub + 1024 * dch:
                                2048 * tsub + 1024 * (dch + 1)]
                    if e_i == 0:
                        ow = [(s_out, 16 * (8 * (s - 1) + tsub + 1))] \
                            if s >= 1 else []
                        ev("vector", lambda e, accsl=accsl, b0=b0,
                           rwcol=rwcol, tsub=tsub: e.tensor_scalar(
                               accsl, PS_B[:, b0:b0 + 1024],
                               rw_all[:, rwcol:rwcol + 1], None, ALU.mult),
                           waits=[(s_pe, pe_d)] + ow,
                           inc=s_v)
                    else:
                        ev("vector", lambda e, accsl=accsl, b0=b0,
                           rwcol=rwcol: e.scalar_tensor_tensor(
                               accsl, PS_B[:, b0:b0 + 1024],
                               rw_all[:, rwcol:rwcol + 1], accsl,
                               ALU.mult, ALU.add),
                           waits=[(s_pe, pe_d)], inc=s_v)
                    if e_i == NE - 1 and dch == 1:
                        acc_fin_hist[(s, tsub)] = sched.n(s_v)
                    rd_fence(region, s_v, sched.n(s_v))
                d_done_pe = sched.n(s_pe)
                g_last_pe[0] = pe_gu

        for tsub in range(8):
            emit_epilogue_tile(NSW - 1, tsub)

        ev("sync", None, waits=[(s_out, sched.n(s_out))])
        sched.check()

        sems = {}
        for name in sorted(sem_names):
            sems[name] = es.enter_context(nc.semaphore(name))
        block = es.enter_context(nc.Block())

        def runner(eng_name):
            def run(engobj):
                for (waits, fn, inc, k) in sched.ev[eng_name]:
                    for (sname, v) in waits:
                        engobj.wait_ge(sems[sname], v)
                    inst = fn(engobj) if fn is not None else None
                    if inc is not None:
                        if inst is None:
                            inst = engobj.nop()
                        inst.then_inc(sems[inc], k)
            return run

        block.sync(runner("sync"))
        block.gpsimd(runner("gpsimd"))
        block.tensor(runner("tensor"))
        block.vector(runner("vector"))
        block.scalar(runner("scalar"))

    return nc


_NC_CACHE = {}


def kernel(**inputs):
    x = np.ascontiguousarray(np.asarray(inputs["L0_attn_x"], dtype=np.float32))
    full = {
        "wq": np.ascontiguousarray(np.asarray(inputs["L0_attn_wq"], np.float32)),
        "wk": np.ascontiguousarray(np.asarray(inputs["L0_attn_wk"], np.float32)),
        "wv": np.ascontiguousarray(np.asarray(inputs["L0_attn_wv"], np.float32)),
        "wo": np.ascontiguousarray(np.asarray(inputs["L0_attn_wo"], np.float32)),
        "bq": np.ascontiguousarray(np.asarray(inputs["L0_attn_bq"], np.float32)),
        "bk": np.ascontiguousarray(np.asarray(inputs["L0_attn_bk"], np.float32)),
        "bv": np.ascontiguousarray(np.asarray(inputs["L0_attn_bv"], np.float32)),
        "bo": np.ascontiguousarray(np.asarray(inputs["L0_attn_bo"], np.float32)),
        "gate": np.ascontiguousarray(np.asarray(inputs["L0_ffn_gate"], np.float32)),
        "gup": np.ascontiguousarray(np.asarray(inputs["L0_ffn_gate_up_proj"], np.float32)),
        "dpj": np.ascontiguousarray(np.asarray(inputs["L0_ffn_down_proj"], np.float32)),
    }
    if "nc" not in _NC_CACHE:
        _NC_CACHE["nc"] = build_program()
    nc = _NC_CACHE["nc"]
    in_maps = []
    for c in range(8):
        m = {"x": np.ascontiguousarray(x[2 * c:2 * c + 2])}
        m.update(full)
        in_maps.append(m)
    res = run_bass_kernel_spmd(nc, in_maps, core_ids=list(range(8)))
    return np.concatenate([res.results[c]["out"] for c in range(8)], axis=0)


# version 09
# speedup vs baseline: 1.0681x; 1.0681x over previous
import contextlib
import numpy as np
import concourse.bass as bass
import concourse.mybir as mybir
from concourse.bass_utils import run_bass_kernel_spmd

F32 = mybir.dt.float32
BF16 = mybir.dt.bfloat16
AF = mybir.ActivationFunctionType
ALU = mybir.AluOpType
AX = mybir.AxisListType

D = 2048
SEQ = 2048
NB = 2
T = NB * SEQ
NH = 16
DH = 128
NE = 32
FI = 768
KC = D // 128
NSW = 4
TSW = T // NSW
ISQ = 1.0 / float(np.sqrt(float(DH)))

ENGINES = ("sync", "gpsimd", "tensor", "vector", "scalar")


class Sched:

    def __init__(self):
        self.ev = {e: [] for e in ENGINES}
        self.cnt = {}

    def n(self, sem):
        return self.cnt.get(sem, 0)

    def add(self, eng, fn, waits=(), inc=None, k=1):
        w = [(s, v) for (s, v) in waits if v > 0]
        after = None
        if inc is not None:
            self.cnt[inc] = self.cnt.get(inc, 0) + k
            after = self.cnt[inc]
        self.ev[eng].append((w, fn, inc, k))
        return after

    def check(self):
        pos = {e: 0 for e in ENGINES}
        val = {}
        progress = True
        while progress:
            progress = False
            for e in ENGINES:
                while pos[e] < len(self.ev[e]):
                    waits, _fn, inc, k = self.ev[e][pos[e]]
                    if any(val.get(s, 0) < v for (s, v) in waits):
                        break
                    if inc is not None:
                        val[inc] = val.get(inc, 0) + k
                    pos[e] += 1
                    progress = True
        stuck = {e: pos[e] for e in ENGINES if pos[e] < len(self.ev[e])}
        if stuck:
            det = []
            for e in stuck:
                waits = self.ev[e][pos[e]][0]
                bad = [(s, v, val.get(s, 0)) for (s, v) in waits
                       if val.get(s, 0) < v]
                det.append(f"{e}@{pos[e]}/{len(self.ev[e])} waits {bad}")
            raise RuntimeError("schedule deadlock: " + "; ".join(det))


def build_program():
    nc = bass.Bass("TRN2", target_bir_lowering=False)

    x_ap = nc.dram_tensor("x", [NB, SEQ, D], F32, kind="ExternalInput").ap()
    wq_ap = nc.dram_tensor("wq", [D, D], F32, kind="ExternalInput").ap()
    wk_ap = nc.dram_tensor("wk", [D, D], F32, kind="ExternalInput").ap()
    wv_ap = nc.dram_tensor("wv", [D, D], F32, kind="ExternalInput").ap()
    wo_ap = nc.dram_tensor("wo", [D, D], F32, kind="ExternalInput").ap()
    bq_ap = nc.dram_tensor("bq", [D], F32, kind="ExternalInput").ap()
    bk_ap = nc.dram_tensor("bk", [D], F32, kind="ExternalInput").ap()
    bv_ap = nc.dram_tensor("bv", [D], F32, kind="ExternalInput").ap()
    bo_ap = nc.dram_tensor("bo", [D], F32, kind="ExternalInput").ap()
    gate_ap = nc.dram_tensor("gate", [D, NE], F32, kind="ExternalInput").ap()
    gup_ap = nc.dram_tensor("gup", [NE, D, 2 * FI], F32, kind="ExternalInput").ap()
    dpj_ap = nc.dram_tensor("dpj", [NE, FI, D], F32, kind="ExternalInput").ap()
    out_ap = nc.dram_tensor("out", [NB, SEQ, D], F32, kind="ExternalOutput").ap()

    ot_sc = nc.dram_tensor("ot_sc", [NB, D, SEQ], BF16, kind="Internal").ap()
    x1_sc = nc.dram_tensor("x1_sc", [T, D], F32, kind="Internal").ap()
    x1t_sc = nc.dram_tensor("x1t_sc", [D, T], BF16, kind="Internal").ap()

    es = contextlib.ExitStack()
    sched = Sched()
    sem_names = set()

    def SEM(name):
        sem_names.add(name)
        return name

    def ev(eng, fn, waits=(), inc=None, k=1):
        return sched.add(eng, fn, waits, inc, k)

    with es:
        R0 = nc.alloc_sbuf_tensor("R0", [128, 32768], BF16)
        R1 = nc.alloc_sbuf_tensor("R1", [128, 16384], F32)
        R2 = nc.alloc_sbuf_tensor("R2", [128, 8192], BF16)
        R3 = nc.alloc_sbuf_tensor("R3", [128, 18432], BF16)

        sb = lambda n, s, d: nc.alloc_sbuf_tensor(n, s, d)
        identf = sb("identf", [128, 128], F32)
        ident_bf = sb("ident_bf", [128, 128], BF16)
        ones_f = sb("ones_f", [1, 128], F32)
        ones_bf = sb("ones_bf", [1, 128], BF16)
        b16 = R1[0:16, 0:128]
        bqTs = sb("bqTs", [128, 16], F32)
        bkT = sb("bkT", [128, 16], F32)
        bvT = sb("bvT", [128, 16], F32)
        gate_sb = sb("gate_sb", [128, KC * NE], BF16)
        rw_all = sb("rw_all", [128, (T // 128) * NE], F32)
        stats = sb("stats", [128, 24], F32)
        P_un = sb("P_un", [128, 2048], BF16)
        P_un3 = sb("P_un3", [128, 2048], BF16)
        diag2 = sb("diag2", [128, 256], BF16)
        OT_sb = sb("OT_sb", [128, 1024], BF16)
        sg_sb = sb("sg_sb", [128, 1024], BF16)
        p_sb = sb("p_sb", [128, NE], F32)
        pwork = sb("pwork", [128, NE], F32)
        wun = sb("wun", [128, NE], F32)
        x1res = sb("x1res", [128, D], F32)

        PS_A = nc.alloc_psum_tensor("PS_A", [128, 2048], F32)
        PS_B = nc.alloc_psum_tensor("PS_B", [128, 2048], F32)
        SCT = (PS_A, PS_B)

        (NEGM0, NEGM1, SSUM0, SSUM1, RS0, RS1, MUS, NEGMU, VARS, LNV, RSTD,
         BIAS2, NEGML, SUME, MX, SWUN, RSW) = range(17)
        st_c = lambda i: stats[:, i:i + 1]

        s_pe = SEM("s_pe")
        s_v = SEM("s_v")
        s_sc = SEM("s_sc")
        s_g = SEM("s_g")
        s_pro = SEM("s_pro")
        s_xr = [SEM(f"s_xr{i}") for i in range(2)]
        s_wr = [SEM(f"s_wr{i}") for i in range(2)]
        s_oti = [SEM(f"s_oti{i}") for i in range(2)]
        s_ot = [SEM(f"s_ot{i}") for i in range(2)]
        s_xz = [SEM(f"s_xz{i}") for i in range(2)]
        s_sx1 = SEM("s_sx1")
        s_sxt = SEM("s_sxt")
        s_xts = SEM("s_xts")
        s_gu = [SEM(f"s_gu{i}") for i in range(4)]
        s_dp = SEM("s_dp")
        s_x1r = SEM("s_x1r")
        s_out = SEM("s_out")

        fence = {}

        def rd_fence(region, sem, cnt):
            for r in (region if isinstance(region, tuple) else (region,)):
                fence[r] = (sem, cnt)

        def get_fence(*regions):
            out = []
            for r in regions:
                if isinstance(r, tuple):
                    out.extend(fence[x] for x in r if x in fence)
                elif r in fence:
                    out.append(fence[r])
            return out

        PA01 = ("A0", "A1")
        PA23 = ("A2", "A3")
        PB01 = ("B0", "B1")
        PB23 = ("B2", "B3")
        PAIR01 = (PA01, PB01)
        PAIR23 = (PA23, PB23)

        def transpose_rows(src_tile, dest_fn, waits_ld):
            pe_last = 0
            for g in range(2):
                region = PA01 if g == 0 else PA23
                def mm(e, g=g, src=src_tile):
                    last = None
                    for j in range(8):
                        dc = 8 * g + j
                        last = e.matmul(
                            PS_A[:, 1024 * g + 128 * j:1024 * g + 128 * (j + 1)],
                            src[:, 128 * dc:128 * (dc + 1)],
                            ident_bf[:, :], start=True, stop=True)
                    return last
                ev("tensor", mm, waits=list(waits_ld) + get_fence(region),
                   inc=s_pe)
                pe = sched.n(s_pe)
                pe_last = pe
                def evac(e, g=g, dest_fn=dest_fn):
                    return e.copy(dest_fn(g),
                                  PS_A[:, 1024 * g:1024 * (g + 1)]
                                  .rearrange("p (k t) -> p k t", t=128))
                ev("scalar", evac, waits=[(s_pe, pe)], inc=s_sc)
                rd_fence(region, s_sc, sched.n(s_sc))
            return pe_last

        ev("gpsimd", lambda e: e.memset(identf[:, :], 0.0), inc=s_g)
        ev("gpsimd", lambda e: e.affine_select(
            out=identf[:, :], in_=identf[:, :], compare_op=ALU.not_equal,
            fill=1.0, base=0, pattern=[[-1, 128]], channel_multiplier=1),
           waits=[(s_g, sched.n(s_g))], inc=s_g)
        ev("gpsimd", lambda e: e.memset(ones_f[:, :], 1.0),
           waits=[(s_g, sched.n(s_g))], inc=s_g)
        g_done = sched.n(s_g)
        ev("scalar", lambda e: e.copy(ident_bf[:, :], identf[:, :]),
           waits=[(s_g, g_done)], inc=s_sc)
        ev("scalar", lambda e: e.copy(ones_bf[:, :], ones_f[:, :]), inc=s_sc)
        sc_idents = sched.n(s_sc)

        for src, dst, scale in ((bq_ap, bqTs, ISQ), (bk_ap, bkT, 1.0),
                                (bv_ap, bvT, 1.0)):
            ev("sync", lambda e, s=src: e.dma_start(
                out=b16[:, :], in_=s.rearrange("(a b) -> a b", b=128)),
               waits=[(s_sc, sched.n(s_sc))], inc=s_pro, k=16)
            ld = sched.n(s_pro)
            ev("tensor", lambda e: e.matmul(
                PS_A[:, 0:16], b16[:, :], identf[0:16, 0:16],
                start=True, stop=True),
               waits=[(s_pro, ld), (s_g, g_done)] + get_fence("A0"), inc=s_pe)
            pe = sched.n(s_pe)
            ev("scalar", lambda e, d=dst, sc=scale: e.activation(
                d[:, :], PS_A[:, 0:16], AF.Identity, bias=0.0, scale=sc),
               waits=[(s_pe, pe)], inc=s_sc)
            rd_fence("A0", s_sc, sched.n(s_sc))
        ev("gpsimd", lambda e: e.dma_start(
            out=gate_sb[:, :].rearrange("p (k e) -> p k e", e=NE),
            in_=gate_ap.rearrange("(k p) e -> p k e", p=128)),
           inc=s_g, k=16)
        pro_done_g = sched.n(s_g)

        XT = R0
        KT = R2[:, 0:2048]
        Vh = R2[:, 2048:4096]
        QT = R2[:, 4096:6144]
        PUN = (P_un, R2[:, 6144:8192], P_un3)
        DIAG = (diag2[:, 0:128], diag2[:, 128:256])
        wT = R3[:, 0:2048]
        xw = [R3[:, 2048:4096], R3[:, 4096:6144]]
        panel = (R3[:, 2048:10240], R3[:, 10240:18432])

        def p1a(b):
            for tc in range(KC):
                buf = tc % 2
                ev("gpsimd", lambda e, b=b, tc=tc, buf=buf: e.dma_start(
                    out=xw[buf][:, :],
                    in_=x_ap[b, 128 * tc:128 * (tc + 1), :]),
                   waits=get_fence(f"xw{buf}", "PAN0")
                   + [(s_sc, sc_idents), (s_xr[buf], sched.n(s_xr[buf]))],
                   inc=s_xr[buf], k=16)
                ld = sched.n(s_xr[buf])
                def dest(g, tc=tc):
                    o = XT[:, :].rearrange("p (k t) -> p k t", t=2048)
                    return o[:, 8 * g:8 * (g + 1), 128 * tc:128 * (tc + 1)]
                pe = transpose_rows(xw[buf], dest,
                                    [(s_xr[buf], ld), (s_sc, sc_idents)]
                                    + get_fence("XTw"))
                rd_fence(f"xw{buf}", s_pe, pe)

        def build_wT(w_ap, h):
            buf = h % 2
            ev("gpsimd", lambda e, h=h, buf=buf, w_ap=w_ap: e.dma_start(
                out=xw[buf][:, :],
                in_=w_ap[128 * h:128 * (h + 1), :]),
               waits=get_fence(f"xw{buf}", "PAN0")
               + [(s_wr[buf], sched.n(s_wr[buf]))],
               inc=s_wr[buf], k=16)
            ld = sched.n(s_wr[buf])
            def dest(g):
                o = wT.rearrange("p (k f) -> p k f", f=128)
                return o[:, 8 * g:8 * (g + 1), :]
            pe = transpose_rows(xw[buf], dest, [(s_wr[buf], ld)]
                                + get_fence("wTw"))
            rd_fence(f"xw{buf}", s_pe, pe)
            return sched.n(s_sc)

        def proj_T(h, bias_t, scale, out_t):
            wT_ready = sched.n(s_sc)
            for tn in range(4):
                rr = tn % 4
                c0 = 512 * rr
                def mm(e, tn=tn, c0=c0):
                    last = None
                    for k in range(KC):
                        last = e.matmul(
                            PS_B[:, c0:c0 + 512],
                            wT[:, 128 * k:128 * (k + 1)],
                            XT[:, 2048 * k + 512 * tn:2048 * k + 512 * (tn + 1)],
                            start=(k == 0), stop=(k == KC - 1))
                    return last
                ev("tensor", mm,
                   waits=[(s_sc, wT_ready)] + get_fence(f"B{rr}"), inc=s_pe)
                pe = sched.n(s_pe)
                def evac(e, tn=tn, c0=c0, h=h, bias_t=bias_t, scale=scale,
                         out_t=out_t):
                    return e.activation(
                        out_t[:, 512 * tn:512 * (tn + 1)],
                        PS_B[:, c0:c0 + 512], AF.Identity,
                        bias=bias_t[:, h:h + 1], scale=scale)
                ev("scalar", evac,
                   waits=[(s_pe, pe)] + get_fence("KVQ"), inc=s_sc)
                rd_fence(f"B{rr}", s_sc, sched.n(s_sc))
            rd_fence("wTw", s_pe, sched.n(s_pe))

        def proj_V(h):
            wT_ready = sched.n(s_sc)
            for tc in range(KC):
                rr = tc % 4
                c0 = 512 * rr
                def mm(e, tc=tc, c0=c0):
                    last = None
                    for k in range(KC):
                        last = e.matmul(
                            PS_B[:, c0:c0 + 128],
                            XT[:, 2048 * k + 128 * tc:2048 * k + 128 * (tc + 1)],
                            wT[:, 128 * k:128 * (k + 1)],
                            start=(k == 0), stop=(k == KC - 1))
                    return last
                ev("tensor", mm,
                   waits=[(s_sc, wT_ready)] + get_fence(f"B{rr}"), inc=s_pe)
                pe = sched.n(s_pe)
                ev("scalar", lambda e, tc=tc, c0=c0: e.copy(
                    Vh[:, 128 * tc:128 * (tc + 1)], PS_B[:, c0:c0 + 128]),
                   waits=[(s_pe, pe)] + get_fence("KVQ"), inc=s_sc)
                rd_fence(f"B{rr}", s_sc, sched.n(s_sc))
            rd_fence("wTw", s_pe, sched.n(s_pe))

        def p1c(b, h):
            kvq_ready = sched.n(s_sc)
            negm_v = {}
            diag_g = {}
            ev01_v = {}
            ev23_sc = {}

            def emit_mms(i):
                p = i % 2
                SC = SCT[p]
                w1 = [(s_sc, kvq_ready)] + get_fence(PAIR01[p])
                def mm01(e, i=i, SC=SC):
                    last = None
                    for kn in range(2):
                        last = e.matmul(
                            SC[:, 512 * kn:512 * (kn + 1)],
                            QT[:, 128 * i:128 * (i + 1)],
                            KT[:, 512 * kn:512 * (kn + 1)],
                            start=True, stop=True)
                    return last
                ev("tensor", mm01, waits=w1, inc=s_pe)
                w2 = get_fence(PAIR23[p])
                def mm23(e, i=i, SC=SC):
                    last = None
                    for kn in range(2, 4):
                        last = e.matmul(
                            SC[:, 512 * kn:512 * (kn + 1)],
                            QT[:, 128 * i:128 * (i + 1)],
                            KT[:, 512 * kn:512 * (kn + 1)],
                            start=True, stop=True)
                    return last
                ev("tensor", mm23, waits=w2, inc=s_pe)
                return sched.n(s_pe)

            def emit_negm(i, mms_pe):
                p = i % 2
                negm_c = NEGM0 if p == 0 else NEGM1
                ev("vector", lambda e, SC=SCT[p], negm_c=negm_c: e.reduce_max(
                    st_c(negm_c), SC[:, :], axis=AX.X, negate=True),
                   waits=[(s_pe, mms_pe)] + get_fence(f"NEGM{p}"), inc=s_v)
                negm_v[i] = sched.n(s_v)

            def emit_exp(i):
                p = i % 2
                r3 = i % 3
                negm_c, ssum_c = (NEGM0, SSUM0) if p == 0 else (NEGM1, SSUM1)
                ev("scalar", lambda e, r3=r3, SC=SCT[p], negm_c=negm_c,
                   ssum_c=ssum_c: e.activation(
                       PUN[r3][:, :], SC[:, :], AF.Exp, bias=st_c(negm_c),
                       scale=1.0, accum_out=st_c(ssum_c)),
                   waits=[(s_v, negm_v[i])]
                   + get_fence(f"PUN{r3}", f"SSUM{p}"),
                   inc=s_sc)
                n_exp = sched.n(s_sc)
                rd_fence(f"NEGM{p}", s_sc, n_exp)
                rd_fence(PAIR01[p], s_sc, n_exp)
                rd_fence(PAIR23[p], s_sc, n_exp)
                return n_exp

            def emit_rcd(i, n_exp):
                p = i % 2
                ssum_c, rs_c = (SSUM0, RS0) if p == 0 else (SSUM1, RS1)
                ev("vector", lambda e, ssum_c=ssum_c, rs_c=rs_c: e.reciprocal(
                    st_c(rs_c), st_c(ssum_c)),
                   waits=[(s_sc, n_exp)] + get_fence(f"RS{p}"), inc=s_v)
                n_rc = sched.n(s_v)
                rd_fence(f"SSUM{p}", s_v, n_rc)
                ev("gpsimd", lambda e, p=p, rs_c=rs_c: e.tensor_scalar(
                    DIAG[p], identf[:, :], st_c(rs_c), None, ALU.mult),
                   waits=[(s_v, n_rc), (s_g, g_done)]
                   + get_fence(f"DIAG{p}"), inc=s_g)
                diag_g[i] = sched.n(s_g)
                rd_fence(f"RS{p}", s_g, diag_g[i])

            def emit_ptg0(j):
                q = (j + 1) % 2
                r3 = j % 3
                def ptg0(e, SC=SCT[q], r3=r3, p=j % 2):
                    last = None
                    for c in range(8):
                        last = e.matmul(
                            SC[:, 128 * c:128 * (c + 1)],
                            PUN[r3][:, 128 * c:128 * (c + 1)],
                            DIAG[p], start=True, stop=True)
                    return last
                ev("tensor", ptg0,
                   waits=[(s_g, diag_g[j])] + get_fence(PAIR01[q]), inc=s_pe)
                return sched.n(s_pe)

            def emit_ev01(j, pe_g0):
                q = (j + 1) % 2
                pan = (j // 4) % 2
                qr = j % 4
                panw = get_fence(f"PAN{pan}")
                if pan == 0:
                    panw = panw + get_fence("xw0", "xw1")
                def ev01(e, SC=SCT[q], pan=pan, qr=qr):
                    o = panel[pan].rearrange("p (kc q) -> p kc q", q=512)
                    return e.tensor_copy(
                        o[:, 0:8, 128 * qr:128 * (qr + 1)],
                        SC[:, 0:1024].rearrange("p (kc q) -> p kc q", q=128))
                ev("vector", ev01, waits=[(s_pe, pe_g0)] + panw, inc=s_v)
                ev01_v[j] = sched.n(s_v)
                rd_fence(PAIR01[q], s_v, ev01_v[j])

            def emit_ptg1(j):
                q = (j + 1) % 2
                r3 = j % 3
                def ptg1(e, SC=SCT[q], r3=r3, p=j % 2):
                    last = None
                    for c in range(8, 16):
                        last = e.matmul(
                            SC[:, 128 * (c - 8):128 * (c - 7)],
                            PUN[r3][:, 128 * c:128 * (c + 1)],
                            DIAG[p], start=True, stop=True)
                    return last
                ev("tensor", ptg1,
                   waits=[(s_g, diag_g[j])] + get_fence(PAIR01[q]), inc=s_pe)
                pe_g1 = sched.n(s_pe)
                rd_fence(f"PUN{j % 3}", s_pe, pe_g1)
                rd_fence(f"DIAG{j % 2}", s_pe, pe_g1)
                return pe_g1

            def emit_ev23(j, pe_g1):
                q = (j + 1) % 2
                pan = (j // 4) % 2
                qr = j % 4
                panw = get_fence(f"PAN{pan}")
                if pan == 0:
                    panw = panw + get_fence("xw0", "xw1")
                def ev23(e, SC=SCT[q], pan=pan, qr=qr):
                    o = panel[pan].rearrange("p (kc q) -> p kc q", q=512)
                    return e.copy(
                        o[:, 8:16, 128 * qr:128 * (qr + 1)],
                        SC[:, 0:1024].rearrange("p (kc q) -> p kc q", q=128))
                ev("scalar", ev23, waits=[(s_pe, pe_g1)] + panw, inc=s_sc)
                ev23_sc[j] = sched.n(s_sc)
                rd_fence(PAIR01[q], s_sc, ev23_sc[j])

            def emit_mmo(a):
                pan = a % 2
                SC = SCT[0]
                def mmo(e, SC=SC, pan=pan):
                    last = None
                    for tc in range(KC):
                        last = e.matmul(
                            SC[:, 1024:1536],
                            Vh[:, 128 * tc:128 * (tc + 1)],
                            panel[pan][:, 512 * tc:512 * (tc + 1)],
                            start=(tc == 0), stop=(tc == KC - 1))
                    return last
                ev("tensor", mmo,
                   waits=[(s_v, ev01_v[4 * a + 3]),
                          (s_sc, ev23_sc[4 * a + 3]),
                          (s_sc, kvq_ready)] + get_fence(PAIR23[0]),
                   inc=s_pe)
                mmo_pe = sched.n(s_pe)
                rd_fence(f"PAN{pan}", s_pe, mmo_pe)
                return mmo_pe

            def emit_ot(a, mmo_pe):
                ping = a % 2
                nst = sched.n(s_oti[ping])
                ev("scalar", lambda e, SC=SCT[0], ping=ping, h=h:
                   e.activation(
                       OT_sb[:, 512 * ping:512 * (ping + 1)],
                       SC[:, 1024:1536],
                       AF.Identity, bias=bvT[:, h:h + 1], scale=1.0),
                   waits=[(s_pe, mmo_pe), (s_oti[ping], nst)], inc=s_sc)
                sc_oe = sched.n(s_sc)
                rd_fence(PAIR23[0], s_sc, sc_oe)
                ev("sync", lambda e, b=b, h=h, a=a, ping=ping: e.dma_start(
                    out=ot_sc[b, 128 * h:128 * (h + 1),
                              512 * a:512 * (a + 1)],
                    in_=OT_sb[:, 512 * ping:512 * (ping + 1)]),
                   waits=[(s_sc, sc_oe)], inc=s_oti[ping], k=16)

            for i in range(16):
                mms_pe = emit_mms(i)
                pe_g0 = emit_ptg0(i - 2) if i >= 2 else None
                emit_negm(i, mms_pe)
                if i >= 2:
                    emit_ev01(i - 2, pe_g0)
                    pe_g1 = emit_ptg1(i - 2)
                n_exp = emit_exp(i)
                if i >= 2:
                    emit_ev23(i - 2, pe_g1)
                mmo_due = i >= 7 and (i - 7) % 4 == 0
                if mmo_due:
                    a = (i - 7) // 4
                    mmo_pe = emit_mmo(a)
                emit_rcd(i, n_exp)
                if mmo_due:
                    emit_ot(a, mmo_pe)
            for j in (14, 15):
                pe_g0 = emit_ptg0(j)
                emit_ev01(j, pe_g0)
                pe_g1 = emit_ptg1(j)
                emit_ev23(j, pe_g1)
            mmo_pe = emit_mmo(3)
            emit_ot(3, mmo_pe)
            rd_fence("KVQ", s_pe, sched.n(s_pe))

        for b in range(NB):
            p1a(b)
            for h in range(NH):
                build_wT(wk_ap, h)
                proj_T(h, bkT, 1.0, KT)
                build_wT(wv_ap, h)
                proj_V(h)
                build_wT(wq_ap, h)
                proj_T(h, bqTs, ISQ, QT)
                p1c(b, h)
            rd_fence("XTw", s_pe, sched.n(s_pe))

        woT = R0
        ot_t = [R2[:, 0:2048], R2[:, 2048:4096]]
        x1b = R2[:, 4096:6144]
        x1t_sb = R2[:, 6144:8192]
        x_res = [R1[:, 0:2048], R1[:, 2048:4096]]
        h_sb = R1[:, 4096:6144]
        x1f = R1[:, 6144:8192]
        borow = R3[0:1, 0:2048]

        def emit_ln(xin, xout, sq_waits, in_ready):
            ev("vector", lambda e: e.reduce_sum(st_c(MUS), xin[:, :],
                                                axis=AX.X),
               waits=list(in_ready), inc=s_v)
            v1 = sched.n(s_v)
            ev("vector", lambda e: e.tensor_scalar(
                st_c(NEGMU), st_c(MUS), -1.0 / D, None, ALU.mult),
               waits=[(s_v, v1)], inc=s_v)
            v2 = sched.n(s_v)
            ev("scalar", lambda e: e.activation(
                xout[:, :], xin[:, :], AF.Square, bias=st_c(NEGMU),
                scale=1.0, accum_out=st_c(VARS)),
               waits=[(s_v, v2)] + list(sq_waits), inc=s_sc)
            s1 = sched.n(s_sc)
            ev("vector", lambda e: e.tensor_scalar(
                st_c(LNV), st_c(VARS), 1.0 / D, 1e-5, ALU.mult, ALU.add),
               waits=[(s_sc, s1)], inc=s_v)
            v3 = sched.n(s_v)
            ev("scalar", lambda e: e.activation(st_c(RSTD), st_c(LNV), AF.Ln),
               waits=[(s_v, v3)], inc=s_sc)
            s2 = sched.n(s_sc)
            ev("scalar", lambda e: e.activation(st_c(RSTD), st_c(RSTD),
                                                AF.Exp, scale=-0.5),
               waits=[(s_sc, s2)], inc=s_sc)
            s3 = sched.n(s_sc)
            ev("vector", lambda e: e.tensor_tensor(
                st_c(BIAS2), st_c(NEGMU), st_c(RSTD), ALU.mult),
               waits=[(s_sc, s3), (s_v, v3)], inc=s_v)
            v4 = sched.n(s_v)
            ev("scalar", lambda e: e.activation(
                xout[:, :], xin[:, :], AF.Identity, bias=st_c(BIAS2),
                scale=st_c(RSTD)),
               waits=[(s_v, v4), (s_sc, s3)], inc=s_sc)
            return sched.n(s_sc)

        ev("gpsimd", lambda e: e.dma_start(out=borow, in_=bo_ap[None, :]),
           waits=[(s_pe, sched.n(s_pe))], inc=s_g, k=16)
        bo_ready = sched.n(s_g)

        for m in range(KC):
            buf = m % 2
            ev("gpsimd", lambda e, m=m, buf=buf: e.dma_start(
                out=xw[buf][:, :], in_=wo_ap[128 * m:128 * (m + 1), :]),
               waits=get_fence(f"xw{buf}", "PAN0")
               + [(s_wr[buf], sched.n(s_wr[buf]))],
               inc=s_wr[buf], k=16)
            ld = sched.n(s_wr[buf])
            def dest(g, m=m):
                o = woT[:, :].rearrange("p (k f) -> p k f", f=2048)
                return o[:, 8 * g:8 * (g + 1), 128 * m:128 * (m + 1)]
            pe = transpose_rows(xw[buf], dest, [(s_wr[buf], ld)])
            rd_fence(f"xw{buf}", s_pe, pe)
        woT_ready = sched.n(s_sc)

        ntile = T // 128
        tile_st = {}

        def p2_stage_a(tt):
            b, trow = tt // (SEQ // 128), 128 * (tt % (SEQ // 128))
            buf = tt % 2
            ev("sync", lambda e, b=b, trow=trow, buf=buf: e.dma_start(
                out=ot_t[buf][:, :].rearrange("p (k q) -> p k q", q=128),
                in_=ot_sc[b, :, trow:trow + 128]
                .rearrange("(k p) q -> p k q", p=128)),
               waits=get_fence(f"ot_t{buf}") + [(s_ot[buf],
                                                 sched.n(s_ot[buf]))],
               inc=s_ot[buf], k=16)
            ld_ot = sched.n(s_ot[buf])
            ev("sync", lambda e, b=b, trow=trow, buf=buf: e.dma_start(
                out=x_res[buf][:, :], in_=x_ap[b, trow:trow + 128, :]),
               waits=get_fence(f"x_res{buf}") + [(s_xz[buf],
                                                  sched.n(s_xz[buf]))],
               inc=s_xz[buf], k=16)
            ld_xr = sched.n(s_xz[buf])
            v_last = 0
            for n in range(4):
                def mm_a(e, buf=buf, n=n):
                    e.matmul(PS_A[:, 512 * n:512 * (n + 1)], ones_bf[0:1, :],
                             borow[0:1, 512 * n:512 * (n + 1)],
                             start=True, stop=False)
                    last = None
                    for k in range(KC):
                        last = e.matmul(
                            PS_A[:, 512 * n:512 * (n + 1)],
                            ot_t[buf][:, 128 * k:128 * (k + 1)],
                            woT[:, 2048 * k + 512 * n:2048 * k + 512 * (n + 1)],
                            start=False, stop=(k == KC - 1))
                    return last
                ev("tensor", mm_a,
                   waits=[(s_ot[buf], ld_ot), (s_sc, woT_ready),
                          (s_g, bo_ready)] + get_fence(f"A{n}"), inc=s_pe)
                pe_a = sched.n(s_pe)
                ev("vector", lambda e, buf=buf, n=n: e.tensor_tensor(
                    h_sb[:, 512 * n:512 * (n + 1)],
                    PS_A[:, 512 * n:512 * (n + 1)],
                    x_res[buf][:, 512 * n:512 * (n + 1)], ALU.add),
                   waits=[(s_pe, pe_a), (s_xz[buf], ld_xr)]
                         + get_fence("h_sb"), inc=s_v)
                v_last = sched.n(s_v)
                rd_fence(f"A{n}", s_v, v_last)
            rd_fence(f"ot_t{buf}", s_pe, sched.n(s_pe))
            rd_fence(f"x_res{buf}", s_v, v_last)
            sc_x1 = emit_ln(h_sb, x1f,
                            sq_waits=get_fence("x1f_dma", "x1f_v"),
                            in_ready=[(s_v, v_last)])
            rd_fence("h_sb", s_sc, sc_x1)
            ev("vector", lambda e: e.tensor_copy(x1b[:, :], x1f[:, :]),
               waits=[(s_sc, sc_x1)] + get_fence("x1b"), inc=s_v)
            v_x1b = sched.n(s_v)
            rd_fence("x1f_v", s_v, v_x1b)
            ev("sync", lambda e, tt=tt: e.dma_start(
                out=x1_sc[128 * tt:128 * (tt + 1), :], in_=x1f[:, :]),
               waits=[(s_sc, sc_x1), (s_sx1, sched.n(s_sx1))],
               inc=s_sx1, k=16)
            rd_fence("x1f_dma", s_sx1, sched.n(s_sx1))
            tile_st[tt] = v_x1b

        def p2_stage_b(tt):
            v_x1b = tile_st.pop(tt)
            for g in range(2):
                region = PB01 if g == 0 else PB23
                def mmt(e, g=g):
                    last = None
                    for j in range(8):
                        dc = 8 * g + j
                        last = e.matmul(
                            PS_B[:, 1024 * g + 128 * j:1024 * g + 128 * (j + 1)],
                            x1b[:, 128 * dc:128 * (dc + 1)],
                            ident_bf[:, :], start=True, stop=True)
                    return last
                ev("tensor", mmt, waits=[(s_v, v_x1b)] + get_fence(region),
                   inc=s_pe)
                pe_t = sched.n(s_pe)
                ev("scalar", lambda e, g=g: e.copy(
                    x1t_sb[:, 1024 * g:1024 * (g + 1)],
                    PS_B[:, 1024 * g:1024 * (g + 1)]),
                   waits=[(s_pe, pe_t)] + get_fence("x1t_dma", "x1t_pe"),
                   inc=s_sc)
                rd_fence(region, s_sc, sched.n(s_sc))
            sc_x1t = sched.n(s_sc)
            rd_fence("x1b", s_pe, sched.n(s_pe))
            ev("sync", lambda e, tt=tt: e.dma_start(
                out=x1t_sc[:, 128 * tt:128 * (tt + 1)]
                .rearrange("(k p) t -> p k t", p=128),
                in_=x1t_sb[:, :].rearrange("p (k t) -> p k t", t=128)),
               waits=[(s_sc, sc_x1t), (s_sxt, sched.n(s_sxt))],
               inc=s_sxt, k=16)
            rd_fence("x1t_dma", s_sxt, sched.n(s_sxt))
            def mm_r(e):
                last = None
                for k in range(KC):
                    last = e.matmul(
                        PS_B[:, 0:NE], x1t_sb[:, 128 * k:128 * (k + 1)],
                        gate_sb[:, NE * k:NE * (k + 1)],
                        start=(k == 0), stop=(k == KC - 1))
                return last
            ev("tensor", mm_r,
               waits=[(s_sc, sc_x1t), (s_g, pro_done_g)] + get_fence("B0"),
               inc=s_pe)
            pe_r = sched.n(s_pe)
            rd_fence("x1t_pe", s_pe, pe_r)
            ev("vector", lambda e: e.reduce_max(
                st_c(NEGML), PS_B[:, 0:NE], axis=AX.X, negate=True),
               waits=[(s_pe, pe_r)], inc=s_v)
            v_ml = sched.n(s_v)
            ev("scalar", lambda e: e.activation(
                p_sb[:, :], PS_B[:, 0:NE], AF.Exp, bias=st_c(NEGML),
                accum_out=st_c(SUME)),
               waits=[(s_v, v_ml)], inc=s_sc)
            sc_pr = sched.n(s_sc)
            rd_fence("B0", s_sc, sc_pr)
            def vstep(fn, extra=()):
                ev("vector", fn, waits=[(s_v, sched.n(s_v))] + list(extra),
                   inc=s_v)
            vstep(lambda e: e.tensor_copy(pwork[:, :], p_sb[:, :]),
                  extra=[(s_sc, sc_pr)])
            for _r in range(4):
                vstep(lambda e: e.reduce_max(st_c(MX), pwork[:, :],
                                             axis=AX.X))
                vstep(lambda e: e.tensor_scalar(
                    wun[:, :], pwork[:, :], st_c(MX), None, ALU.is_equal))
                vstep(lambda e: e.scalar_tensor_tensor(
                    pwork[:, :], wun[:, :], -2.0, pwork[:, :],
                    ALU.mult, ALU.add))
            vstep(lambda e: e.scalar_tensor_tensor(
                wun[:, :], p_sb[:, :], st_c(MX), p_sb[:, :],
                ALU.is_ge, ALU.mult))
            vstep(lambda e: e.reduce_sum(st_c(SWUN), wun[:, :], axis=AX.X))
            vstep(lambda e: e.reciprocal(st_c(RSW), st_c(SWUN)))
            vstep(lambda e, tt=tt: e.tensor_scalar(
                rw_all[:, NE * tt:NE * (tt + 1)], wun[:, :], st_c(RSW),
                None, ALU.mult))

        for tt in range(ntile):
            p2_stage_a(tt)
            if tt >= 1:
                p2_stage_b(tt - 1)
        p2_stage_b(ntile - 1)

        ACC = R1
        dp_sb = R0[:, 0:12288]
        gup_buf = [R0[:, 12288 + 2048 * i:12288 + 2048 * (i + 1)]
                   for i in range(4)]
        Hh = R0[:, 20480:26624]
        X1T_s = R3[:, 0:8192]
        X1T_s2 = R2[:, 0:8192]

        def x1t_chunk(k):
            if k < 8:
                return X1T_s[:, 1024 * k:1024 * (k + 1)]
            return X1T_s2[:, 1024 * (k - 8):1024 * (k - 7)]

        p3_pe = sched.n(s_pe)
        p3_sc = sched.n(s_sc)
        p3_v = sched.n(s_v)
        p3_sx1 = sched.n(s_sx1)
        arena_free = [(s_pe, p3_pe), (s_sc, p3_sc), (s_v, p3_v),
                      (s_sx1, p3_sx1), (s_sxt, sched.n(s_sxt))]

        acc_fin_hist = {}
        g_last_pe = [0]
        x1res_free = [0]
        slot_free_pe = [0, 0, 0, 0]
        d_done_pe = p3_pe
        first_loads = True

        def emit_epilogue_tile(se, tsub):
            eb, er0 = se // 2, TSW * (se % 2)
            ev("sync", lambda e, se=se, tsub=tsub: e.dma_start(
                out=x1res[:, :],
                in_=x1_sc[TSW * se + 128 * tsub:
                          TSW * se + 128 * (tsub + 1), :]),
               waits=[(s_x1r, sched.n(s_x1r)), (s_sc, x1res_free[0]),
                      (s_sx1, 16 * (8 * se + tsub + 1))],
               inc=s_x1r, k=16)
            ld_r = sched.n(s_x1r)
            accsl = ACC[:, 2048 * tsub:2048 * (tsub + 1)]
            ev("vector", lambda e, accsl=accsl: e.tensor_tensor(
                x1res[:, :], x1res[:, :], accsl, ALU.add),
               waits=[(s_x1r, ld_r), (s_v, acc_fin_hist[(se, tsub)])],
               inc=s_v)
            v_add = sched.n(s_v)
            sc_o = emit_ln(x1res, accsl, sq_waits=[],
                           in_ready=[(s_v, v_add)])
            x1res_free[0] = sc_o
            ev("sync", lambda e, eb=eb, er0=er0, tsub=tsub, accsl=accsl:
               e.dma_start(
                   out=out_ap[eb, er0 + 128 * tsub:
                              er0 + 128 * (tsub + 1), :],
                   in_=accsl),
               waits=[(s_sc, sc_o)], inc=s_out, k=16)
            assert sched.n(s_out) == 16 * (8 * se + tsub + 1)

        for s in range(NSW):
            b, r0 = s // 2, TSW * (s % 2)
            war = list(arena_free) + [(s_pe, g_last_pe[0]),
                                      (s_xts, sched.n(s_xts))]
            xts_base = sched.n(s_xts)
            for kk in range(KC):
                ev("sync", lambda e, s=s, kk=kk: e.dma_start(
                    out=x1t_chunk(kk),
                    in_=x1t_sc[128 * kk:128 * (kk + 1),
                               TSW * s:TSW * (s + 1)]),
                   waits=war if kk == 0 else [], inc=s_xts, k=16)
            ld_xts = sched.n(s_xts)
            for e_i in range(NE):
                ldg = {}
                fine = (e_i == 0)
                per_unit = 16 if fine else 1
                pe_base2 = [0]

                def g_done_cnt(m):
                    return pe_base2[0] + per_unit * (m + 1)

                def load_chunk(c, e_i=e_i):
                    ri = c % 4
                    fcol = 128 * (c // 2) + FI * (c % 2)
                    if c < 4:
                        slot_war = slot_free_pe[ri]
                    else:
                        slot_war = g_done_cnt(c - 4)
                    w = [(s_pe, slot_war),
                         (s_gu[ri], sched.n(s_gu[ri]))]
                    if first_loads:
                        w = w + arena_free
                    ev("gpsimd", lambda e, e_i=e_i, fcol=fcol, ri=ri:
                       e.dma_start(
                           out=gup_buf[ri][:, :]
                           .rearrange("p (k f) -> p k f", f=128),
                           in_=gup_ap[e_i, :, fcol:fcol + 128]
                           .rearrange("(k p) f -> p k f", p=128)),
                       waits=w, inc=s_gu[ri], k=16)
                    ldg[c] = sched.n(s_gu[ri])

                for c in range(4):
                    load_chunk(c)
                dpw = [(s_pe, d_done_pe), (s_dp, sched.n(s_dp))]
                if first_loads:
                    dpw = dpw + arena_free
                    first_loads = False
                ev("gpsimd", lambda e, e_i=e_i: e.dma_start(
                    out=dp_sb[:, :].rearrange("p (j d) -> p j d", d=D),
                    in_=dpj_ap[e_i].rearrange("(j p) d -> p j d", p=128)),
                   waits=dpw, inc=s_dp, k=16)
                ld_dp = sched.n(s_dp)

                def emit_g_unit(unit, psl, ld_cnt, ri, fine):
                    if not fine:
                        def mm(e, psl=psl, ri=ri):
                            last = None
                            for th in range(2):
                                for k in range(KC):
                                    last = e.matmul(
                                        psl[:, 512 * th:512 * (th + 1)],
                                        gup_buf[ri][:, 128 * k:128 * (k + 1)],
                                        x1t_chunk(k)[:, 512 * th:
                                                     512 * (th + 1)],
                                        start=(k == 0), stop=(k == KC - 1))
                            return last
                        ev("tensor", mm,
                           waits=[(s_gu[ri], ld_cnt), (s_xts, ld_xts)]
                                 + get_fence(f"G{unit}"), inc=s_pe)
                    else:
                        for k in range(KC):
                            w = [(s_xts, xts_base + 16 * (k + 1))]
                            if k == 0:
                                w += [(s_gu[ri], ld_cnt)] \
                                    + get_fence(f"G{unit}")
                            def mm(e, psl=psl, ri=ri, k=k):
                                last = None
                                for th in range(2):
                                    last = e.matmul(
                                        psl[:, 512 * th:512 * (th + 1)],
                                        gup_buf[ri][:, 128 * k:128 * (k + 1)],
                                        x1t_chunk(k)[:, 512 * th:
                                                     512 * (th + 1)],
                                        start=(k == 0), stop=(k == KC - 1))
                                return last
                            ev("tensor", mm, waits=w, inc=s_pe)
                    return sched.n(s_pe)

                pe_base2[0] = sched.n(s_pe)
                for c in range(4, 12):
                    load_chunk(c)

                for j in range(6):
                    rig, riu = (2 * j) % 4, (2 * j + 1) % 4
                    pe_gg = emit_g_unit("g", PS_A[:, 0:1024], ldg[2 * j],
                                        rig, fine)
                    assert pe_gg == g_done_cnt(2 * j)
                    slot_free_pe[rig] = pe_gg
                    ev("scalar", lambda e: e.activation(
                        sg_sb[:, :], PS_A[:, 0:1024], AF.Silu),
                       waits=[(s_pe, pe_gg)] + get_fence("sg"), inc=s_sc)
                    sc_si = sched.n(s_sc)
                    rd_fence("Gg", s_sc, sc_si)
                    pe_gu = emit_g_unit("u", PS_A[:, 1024:2048],
                                        ldg[2 * j + 1], riu, fine)
                    assert pe_gu == g_done_cnt(2 * j + 1)
                    slot_free_pe[riu] = pe_gu
                    ev("vector", lambda e, j=j: e.scalar_tensor_tensor(
                        Hh[:, 1024 * j:1024 * (j + 1)],
                        PS_A[:, 1024:2048], 1.0, sg_sb[:, :],
                        ALU.mult, ALU.mult),
                       waits=[(s_sc, sc_si), (s_pe, pe_gu)], inc=s_v)
                    v_h = sched.n(s_v)
                    rd_fence("Gu", s_v, v_h)
                    rd_fence("sg", s_v, v_h)
                h_ready = sched.n(s_v)

                for u in range(16):
                    tsub, dch = u // 2, u % 2
                    par = u % 2
                    b0 = 1024 * par
                    region = f"D{par}"
                    def mm_d(e, tsub=tsub, dch=dch, b0=b0):
                        last = None
                        for j in range(6):
                            for nn in range(2):
                                last = e.matmul(
                                    PS_B[:, b0 + 512 * nn:b0 + 512 * (nn + 1)],
                                    Hh[:, 1024 * j + 128 * tsub:
                                       1024 * j + 128 * (tsub + 1)],
                                    dp_sb[:, 2048 * j + 1024 * dch + 512 * nn:
                                          2048 * j + 1024 * dch + 512 * (nn + 1)],
                                    start=(j == 0), stop=(j == 5))
                        return last
                    dwaits = [(s_dp, ld_dp), (s_v, h_ready)]
                    if e_i == 0 and s == 0:
                        dwaits = dwaits + [(s_sc, p3_sc)]
                    ev("tensor", mm_d, waits=dwaits + get_fence(region),
                       inc=s_pe)
                    pe_d = sched.n(s_pe)
                    rwcol = NE * (8 * s + tsub) + e_i
                    accsl = ACC[:, 2048 * tsub + 1024 * dch:
                                2048 * tsub + 1024 * (dch + 1)]
                    if e_i == 0:
                        ow = [(s_out, 16 * (8 * (s - 1) + tsub + 1))] \
                            if s >= 1 else []
                        ev("vector", lambda e, accsl=accsl, b0=b0,
                           rwcol=rwcol, tsub=tsub: e.tensor_scalar(
                               accsl, PS_B[:, b0:b0 + 1024],
                               rw_all[:, rwcol:rwcol + 1], None, ALU.mult),
                           waits=[(s_pe, pe_d)] + ow,
                           inc=s_v)
                    else:
                        ev("vector", lambda e, accsl=accsl, b0=b0,
                           rwcol=rwcol: e.scalar_tensor_tensor(
                               accsl, PS_B[:, b0:b0 + 1024],
                               rw_all[:, rwcol:rwcol + 1], accsl,
                               ALU.mult, ALU.add),
                           waits=[(s_pe, pe_d)], inc=s_v)
                    if e_i == NE - 1 and dch == 1:
                        acc_fin_hist[(s, tsub)] = sched.n(s_v)
                    rd_fence(region, s_v, sched.n(s_v))
                d_done_pe = sched.n(s_pe)
                g_last_pe[0] = pe_gu

            for tsub in range(8):
                emit_epilogue_tile(s, tsub)

        ev("sync", None, waits=[(s_out, sched.n(s_out))])
        sched.check()

        sems = {}
        for name in sorted(sem_names):
            sems[name] = es.enter_context(nc.semaphore(name))
        block = es.enter_context(nc.Block())

        def runner(eng_name):
            def run(engobj):
                for (waits, fn, inc, k) in sched.ev[eng_name]:
                    for (sname, v) in waits:
                        engobj.wait_ge(sems[sname], v)
                    inst = fn(engobj) if fn is not None else None
                    if inc is not None:
                        if inst is None:
                            inst = engobj.nop()
                        inst.then_inc(sems[inc], k)
            return run

        block.sync(runner("sync"))
        block.gpsimd(runner("gpsimd"))
        block.tensor(runner("tensor"))
        block.vector(runner("vector"))
        block.scalar(runner("scalar"))

    return nc


_NC_CACHE = {}


def kernel(**inputs):
    x = np.ascontiguousarray(np.asarray(inputs["L0_attn_x"], dtype=np.float32))
    full = {
        "wq": np.ascontiguousarray(np.asarray(inputs["L0_attn_wq"], np.float32)),
        "wk": np.ascontiguousarray(np.asarray(inputs["L0_attn_wk"], np.float32)),
        "wv": np.ascontiguousarray(np.asarray(inputs["L0_attn_wv"], np.float32)),
        "wo": np.ascontiguousarray(np.asarray(inputs["L0_attn_wo"], np.float32)),
        "bq": np.ascontiguousarray(np.asarray(inputs["L0_attn_bq"], np.float32)),
        "bk": np.ascontiguousarray(np.asarray(inputs["L0_attn_bk"], np.float32)),
        "bv": np.ascontiguousarray(np.asarray(inputs["L0_attn_bv"], np.float32)),
        "bo": np.ascontiguousarray(np.asarray(inputs["L0_attn_bo"], np.float32)),
        "gate": np.ascontiguousarray(np.asarray(inputs["L0_ffn_gate"], np.float32)),
        "gup": np.ascontiguousarray(np.asarray(inputs["L0_ffn_gate_up_proj"], np.float32)),
        "dpj": np.ascontiguousarray(np.asarray(inputs["L0_ffn_down_proj"], np.float32)),
    }
    if "nc" not in _NC_CACHE:
        _NC_CACHE["nc"] = build_program()
    nc = _NC_CACHE["nc"]
    in_maps = []
    for c in range(8):
        m = {"x": np.ascontiguousarray(x[2 * c:2 * c + 2])}
        m.update(full)
        in_maps.append(m)
    res = run_bass_kernel_spmd(nc, in_maps, core_ids=list(range(8)))
    return np.concatenate([res.results[c]["out"] for c in range(8)], axis=0)
